# version 01
# baseline (speedup 1.0000x reference)
import math
import contextlib
import numpy as np
import ml_dtypes
import concourse.bass as bass
import concourse.mybir as mybir
from concourse.bass_utils import run_bass_kernel_spmd

F32 = mybir.dt.float32
BF16 = mybir.dt.bfloat16
AF = mybir.ActivationFunctionType
ALU = mybir.AluOpType
AX = mybir.AxisListType
BF = ml_dtypes.bfloat16

T = 2048
D = 1024
TT = T // 128
DC = D // 128
NEG = -1.0e30



class PB:
    def __init__(self):
        nc = bass.Bass("TRN2", target_bir_lowering=False, debug=False)
        self.nc = nc
        self.st = contextlib.ExitStack()
        e = self.st.enter_context
        self.sem = {k: e(nc.semaphore("sem_" + k)) for k in ("t", "v", "s")}
        self.dsem = e(nc.semaphore("sem_d"))
        self.cnt = {"t": 0, "v": 0, "s": 0}
        self.dcnt = 0
        self.lw = {}
        self.EN = {"t": nc.tensor, "v": nc.vector, "s": nc.scalar, "y": nc.sync}

    def sb(self, name, shape, dt):
        return self.st.enter_context(self.nc.sbuf_tensor(name, shape, dt))

    def ps(self, name, shape, dt):
        return self.st.enter_context(self.nc.psum_tensor(name, shape, dt))

    def _pre(self, ek):
        eng = self.EN[ek]
        for k in ("t", "v", "s"):
            if ek == k and ek == "t":
                continue
            tgt = self.cnt[k]
            if tgt > self.lw.get((ek, k), 0):
                eng.wait_ge(self.sem[k], tgt)
                self.lw[(ek, k)] = tgt
        if self.dcnt > self.lw.get((ek, "d"), 0):
            eng.wait_ge(self.dsem, 16 * self.dcnt)
            self.lw[(ek, "d")] = self.dcnt

    def op(self, ek, inst):
        inst.then_inc(self.sem[ek], 1)
        self.cnt[ek] += 1

    def mm(self, out, lhsT, rhs, start, stop):
        self._pre("t")
        self.op("t", self.nc.tensor.matmul(out, lhsT, rhs, start=start, stop=stop))

    def v(self, fname, *a, **kw):
        self._pre("v")
        self.op("v", getattr(self.nc.vector, fname)(*a, **kw))

    def s(self, fname, *a, **kw):
        self._pre("s")
        self.op("s", getattr(self.nc.scalar, fname)(*a, **kw))

    def dma(self, dst, src):
        self._pre("y")
        self.nc.sync.dma_start(dst, src).then_inc(self.dsem, 16)
        self.dcnt += 1

    def finish(self):
        self._pre("y")
        if self.dcnt > self.lw.get(("y", "d"), 0):
            self.nc.sync.wait_ge(self.dsem, 16 * self.dcnt)
            self.lw[("y", "d")] = self.dcnt


def sliding_ap(dram2d, row, npart, ncols):
    a = dram2d[row:row + 1, 0:ncols].copy()
    a.ap[0] = [1, npart]
    return a



class Bufs:
    def __init__(self, P):
        self.YACC = P.sb("YACC", [128, TT * D], F32)
        self.XTILE = P.sb("XTILE", [128, D], F32)
        self.SQT = P.sb("SQT", [128, D], F32)
        self.XT = P.sb("XT", [128, DC * T], BF16)
        self.A = P.sb("A", [128, 8 * T], BF16)
        self.WB = P.sb("WB", [128, 8192], BF16)
        self.STAT = P.sb("STAT", [128, 64], F32)
        self.SC = P.sb("SC", [128, TT * 16], F32)
        self.SCB = P.sb("SCB", [128, TT * 16], BF16)
        self.BPP = P.sb("BPP", [128, 64], F32)
        self.BPP2 = P.sb("BPP2", [128, 64], F32)
        self.BBC = P.sb("BBC", [128, D], F32)
        self.TMPV = P.sb("TMPV", [128, 512], F32)
        self.IDB = P.sb("IDB", [128, 128], BF16)
        self.FLIPB = P.sb("FLIPB", [128, 128], BF16)
        self.CM4 = P.sb("CM4", [128, 2048], BF16)
        self.ONESB = P.sb("ONESB", [1, 128], BF16)
        self.S32 = P.sb("S32", [32, 4096], BF16)
        self.S32F = P.sb("S32F", [32, 2048], F32)
        self.STG = P.sb("STG", [128, 1024], F32)
        self.STGW = P.sb("STGW", [128, 1024], F32)
        self.PSB = P.ps("PSB", [128, 2048], F32)
        self.PS1 = P.ps("PS1", [128, 512], F32)
        self.PS2 = P.ps("PS2", [128, 512], F32)
        self.PS3 = P.ps("PS3", [128, 512], F32)
        self.PS4 = P.ps("PS4", [128, 512], F32)


def load_consts(P, B, W):
    P.dma(B.IDB[:], W["c_id"])
    P.dma(B.FLIPB[:], W["c_flip"])
    P.dma(B.CM4[:], W["c_cm4"])
    P.dma(B.ONESB[:], W["c_ones"])



def layer_norm_to_XT(P, B, x_ap):
    xnb = B.A[:, 7 * T:7 * T + D]
    for tc in range(TT):
        P.dma(B.XTILE[:], x_ap[tc * 128:(tc + 1) * 128, :])
        st = B.STAT
        P.v("reduce_sum", st[:, 0:1], B.XTILE[:], axis=AX.X)
        P.s("activation", B.SQT[:], B.XTILE[:], AF.Square)
        P.v("reduce_sum", st[:, 1:2], B.SQT[:], axis=AX.X)
        P.v("tensor_scalar_mul", st[:, 2:3], st[:, 0:1], 1.0 / D)
        P.v("tensor_scalar_mul", st[:, 3:4], st[:, 1:2], 1.0 / D)
        P.v("tensor_mul", st[:, 4:5], st[:, 2:3], st[:, 2:3])
        P.v("tensor_sub", st[:, 5:6], st[:, 3:4], st[:, 4:5])
        P.v("tensor_scalar_add", st[:, 5:6], st[:, 5:6], 1e-5)
        P.s("sqrt", st[:, 6:7], st[:, 5:6])
        P.v("reciprocal", st[:, 7:8], st[:, 6:7])
        P.v("tensor_mul", st[:, 8:9], st[:, 2:3], st[:, 7:8])
        P.v("tensor_scalar_mul", st[:, 9:10], st[:, 8:9], -1.0)
        P.s("activation", xnb, B.XTILE[:], AF.Identity, bias=st[:, 9:10],
            scale=st[:, 7:8])
        for dc in range(DC):
            P.mm(B.PS3[:, 0:128], xnb[:, dc * 128:(dc + 1) * 128], B.IDB[:],
                 start=True, stop=True)
            P.s("copy", B.XT[:, dc * T + tc * 128: dc * T + (tc + 1) * 128],
                B.PS3[:, 0:128])


def stage_wT(P, B, w5_ap, mc, kc_n):
    src = w5_ap[mc].rearrange("(kc p) c -> p kc c", p=128)
    dst = B.WB[:, 0:kc_n * 128].rearrange("p (kc c) -> p kc c", c=128)
    P.dma(B.STG[:, 0:kc_n * 128].rearrange("p (kc c) -> p kc c", c=128), src)
    P.v("tensor_copy", dst, B.STG[:, 0:kc_n * 128])


def gemm_T(P, B, w5_ap, m_chunks, kc_n, out_sb, out_col, act=AF.Identity,
           bias_cols=None, bias_sb=None, scale=1.0, rhs_sb=None, rhs_col0=0):
    if rhs_sb is None:
        rhs_sb, rhs_col0 = B.XT, 0
    for mc in range(m_chunks):
        stage_wT(P, B, w5_ap, mc, kc_n)
        for t4 in range(4):
            for kc in range(kc_n):
                P.mm(B.PSB[:, t4 * 512:(t4 + 1) * 512],
                     B.WB[:, kc * 128:(kc + 1) * 128],
                     rhs_sb[:, rhs_col0 + kc * T + t4 * 512:
                            rhs_col0 + kc * T + (t4 + 1) * 512],
                     start=(kc == 0), stop=(kc == kc_n - 1))
            kw = {}
            if bias_sb is not None:
                kw["bias"] = bias_sb[:, bias_cols + mc: bias_cols + mc + 1]
            if scale != 1.0:
                kw["scale"] = scale
            P.s("activation",
                out_sb[:, out_col + mc * T + t4 * 512:
                       out_col + mc * T + (t4 + 1) * 512],
                B.PSB[:, t4 * 512:(t4 + 1) * 512], act, **kw)


def gemmN_acc_group(P, B, lhsT_sb, lcol_fn, w_dram, krange, post):
    n = len(krange)
    for i, kc in enumerate(krange):
        P.dma(B.STGW[:], w_dram[kc * 128:(kc + 1) * 128, :])
        P.v("tensor_copy", B.WB[:, i * 1024:(i + 1) * 1024], B.STGW[:])
    for tc in range(TT):
        for fc in range(2):
            for i, kc in enumerate(krange):
                P.mm(B.PS2[:, :],
                     lhsT_sb[:, lcol_fn(kc, tc):lcol_fn(kc, tc) + 128],
                     B.WB[:, i * 1024 + fc * 512: i * 1024 + (fc + 1) * 512],
                     start=(i == 0), stop=(i == n - 1))
            post(tc, fc, B.PS2[:, :])


def acc_plain(P, B):
    def post(tc, fc, psum):
        yc = B.YACC[:, tc * D + fc * 512: tc * D + (fc + 1) * 512]
        P.v("tensor_add", yc, yc, psum)
    return post


def acc_scaled_ap(P, B, scale_ap_fn):
    def post(tc, fc, psum):
        yc = B.YACC[:, tc * D + fc * 512: tc * D + (fc + 1) * 512]
        P.v("tensor_scalar_mul", B.TMPV[:], psum, scale_ap_fn(tc))
        P.v("tensor_add", yc, yc, B.TMPV[:])
    return post


def residual_store(P, B, x_ap, y_ap, scale, bias_bc=None):
    for tc in range(TT):
        P.dma(B.XTILE[:], x_ap[tc * 128:(tc + 1) * 128, :])
        ya = B.YACC[:, tc * D:(tc + 1) * D]
        if bias_bc is not None:
            P.v("tensor_add", B.SQT[:], ya, bias_bc)
            P.v("scalar_tensor_tensor", B.SQT[:], B.SQT[:], float(scale),
                B.XTILE[:], ALU.mult, ALU.add)
        else:
            P.v("scalar_tensor_tensor", B.SQT[:], ya, float(scale),
                B.XTILE[:], ALU.mult, ALU.add)
        P.dma(y_ap[tc * 128:(tc + 1) * 128, :], B.SQT[:])


def topk2_row(P, B, src_ap, dst_ap, nE, prow=128):
    st = B.STAT
    t = B.TMPV
    P.v("tensor_copy", t[0:prow, 0:nE], src_ap)
    P.v("reduce_max", st[0:prow, 20:21], t[0:prow, 0:nE], axis=AX.X, negate=True)
    P.v("tensor_scalar_mul", st[0:prow, 21:22], st[0:prow, 20:21], -1.0)
    P.v("tensor_scalar", t[0:prow, 32:32 + nE], t[0:prow, 0:nE],
        st[0:prow, 21:22], None, ALU.is_ge)
    P.v("scalar_tensor_tensor", t[0:prow, 64:64 + nE], t[0:prow, 32:32 + nE],
        NEG, t[0:prow, 0:nE], ALU.mult, ALU.add)
    P.v("reduce_max", st[0:prow, 22:23], t[0:prow, 64:64 + nE], axis=AX.X)
    P.s("activation", t[0:prow, 96:96 + nE], t[0:prow, 0:nE], AF.Exp,
        bias=st[0:prow, 20:21])
    P.v("tensor_scalar", t[0:prow, 128:128 + nE], t[0:prow, 0:nE],
        st[0:prow, 22:23], None, ALU.is_ge)
    P.v("tensor_mul", t[0:prow, 160:160 + nE], t[0:prow, 96:96 + nE],
        t[0:prow, 128:128 + nE])
    P.v("reduce_sum", st[0:prow, 23:24], t[0:prow, 160:160 + nE], axis=AX.X)
    P.v("reciprocal", st[0:prow, 24:25], st[0:prow, 23:24])
    P.v("tensor_scalar_mul", dst_ap, t[0:prow, 160:160 + nE],
        st[0:prow, 24:25])


def rope_apply(P, B, cos_sb, sin_sb, qt, scale):
    rot = B.A[:, 4 * T:5 * T]
    if scale != 1.0:
        P.v("tensor_scalar_mul", qt, qt, float(scale))
    P.s("mul", rot[0:32, :], qt[32:64, :], -1.0)
    P.s("copy", rot[32:64, :], qt[0:32, :])
    P.s("mul", rot[64:96, :], qt[96:128, :], -1.0)
    P.s("copy", rot[96:128, :], qt[64:96, :])
    P.v("tensor_mul", qt, qt, cos_sb)
    P.v("tensor_mul", rot[:, :], rot[:, :], sin_sb)
    P.v("tensor_add", qt, qt, rot[:, :])



def attention(P, B, x_ap, y_ap, W, kind):
    A = B.A
    QT2 = A[:, 0 * T:1 * T]
    KT2 = A[:, 1 * T:2 * T]
    VN = A[:, 2 * T:3 * T]
    OT2 = A[:, 3 * T:4 * T]
    PBF = A[:, 4 * T:5 * T]
    PTB = A[:, 5 * T:6 * T]
    SLD = A[:, 6 * T:7 * T]
    VT2 = A[:, 7 * T:8 * T]
    causal = kind != "l0"
    scale = 0.125

    layer_norm_to_XT(P, B, x_ap)
    P.v("memset", B.YACC[:], 0.0)

    if kind == "l2":
        P.dma(B.S32F[0:32, 0:16], W["rel_emb"])
        P.v("tensor_copy", B.S32[0:32, 0:16], B.S32F[0:32, 0:16])
        P.dma(A[0:32, 0:4096], W["ohrevT"])
        for fc in range(8):
            P.mm(B.PS4[0:16, :], B.S32[0:32, 0:16],
                 A[0:32, fc * 512:(fc + 1) * 512], start=True, stop=True)
            P.s("copy", B.S32[0:16, fc * 512:(fc + 1) * 512], B.PS4[0:16, :])
        P.dma(W["rtab"][:, :], B.S32[0:16, :])
    if kind == "l3":
        for hf in range(2):
            P.dma(B.S32F[0:16, :], W["r3rev"][:, hf * 2048:(hf + 1) * 2048])
            P.v("tensor_copy", B.S32[0:16, hf * 2048:(hf + 1) * 2048],
                B.S32F[0:16, :])
        P.dma(W["rtab"][:, :], B.S32[0:16, :])

    if kind == "l0":
        P.dma(B.BPP[:, 32:34], W["bl_pp"])
        gemm_T(P, B, W["wl5"], 2, DC, B.WB, 4096, act=AF.Identity,
               bias_cols=32, bias_sb=B.BPP)
    if kind != "l1":
        P.dma(B.BPP[:, 0:DC], W["bv_pp"])
        P.dma(B.BPP[:, 8:8 + DC], W["bk_pp"])
        P.dma(B.BPP2[:, 0:DC], W["bq_pp"])
        P.v("tensor_scalar_mul", B.BPP[:, 16:16 + DC], B.BPP2[:, 0:DC], scale)
    if kind == "l1":
        P.dma(B.WB[:, 4096:4096 + T], W["cosb"])
        P.dma(B.WB[:, 4096 + T:4096 + 2 * T], W["sinb"])

    for hp in range(8):
        if kind == "l1":
            gemm_T(P, B, W["wq5"][hp:hp + 1], 1, DC, A, 0 * T)
            rope_apply(P, B, B.WB[:, 4096:4096 + T],
                       B.WB[:, 4096 + T:4096 + 2 * T], QT2[:, :], scale)
            gemm_T(P, B, W["wk5"][hp:hp + 1], 1, DC, A, 1 * T)
            rope_apply(P, B, B.WB[:, 4096:4096 + T],
                       B.WB[:, 4096 + T:4096 + 2 * T], KT2[:, :], 1.0)
            gemm_T(P, B, W["wv5"][hp:hp + 1], 1, DC, A, 7 * T)
        elif kind == "l0":
            gemm_T(P, B, W["wq5"][hp:hp + 1], 1, DC, A, 0 * T,
                   bias_cols=16 + hp, bias_sb=B.BPP, scale=scale)
            gemm_T(P, B, W["wk5"][hp:hp + 1], 1, 2, A, 1 * T,
                   bias_cols=8 + hp, bias_sb=B.BPP, rhs_sb=B.WB, rhs_col0=4096)
            gemm_T(P, B, W["wv5"][hp:hp + 1], 1, 2, A, 7 * T,
                   bias_cols=0 + hp, bias_sb=B.BPP, rhs_sb=B.WB, rhs_col0=4096)
        else:
            gemm_T(P, B, W["wq5"][hp:hp + 1], 1, DC, A, 0 * T,
                   bias_cols=16 + hp, bias_sb=B.BPP, scale=scale)
            gemm_T(P, B, W["wk5"][hp:hp + 1], 1, DC, A, 1 * T,
                   bias_cols=8 + hp, bias_sb=B.BPP)
            gemm_T(P, B, W["wv5"][hp:hp + 1], 1, DC, A, 7 * T,
                   bias_cols=0 + hp, bias_sb=B.BPP)
        for tc in range(TT):
            P.mm(B.PS3[:, 0:128], VT2[:, tc * 128:(tc + 1) * 128], B.IDB[:],
                 start=True, stop=True)
            P.s("copy", VN[:, tc * 128:(tc + 1) * 128], B.PS3[:, 0:128])

        for h2 in range(2):
            h = hp * 2 + h2
            poff = h2 * 64
            if kind in ("l2", "l3"):
                P.dma(SLD[:, :], sliding_ap(W["rtab"], h, 128, 2048))
            for b in range(TT):
                jn = (b + 1) * 128 if causal else T
                nch = (jn + 511) // 512
                for jc in range(nch):
                    j0 = jc * 512
                    ln = min(512, jn - j0)
                    pslice = B.PSB[:, j0:j0 + ln]
                    first = True
                    if causal and jc == nch - 1:
                        v = ln // 128 - 1
                        P.mm(pslice, B.FLIPB[:],
                             B.CM4[:, v * 512:v * 512 + ln],
                             start=True, stop=False)
                        first = False
                    if kind in ("l2", "l3"):
                        w0 = 1920 - 128 * b + j0
                        P.mm(pslice, B.FLIPB[:], SLD[:, w0:w0 + ln],
                             start=first, stop=False)
                        first = False
                    P.mm(pslice, QT2[poff:poff + 64, b * 128:(b + 1) * 128],
                         KT2[poff:poff + 64, j0:j0 + ln],
                         start=first, stop=True)
                st = B.STAT
                P.v("reduce_max", st[:, 30:31], B.PSB[:, 0:jn], axis=AX.X,
                    negate=True)
                P.s("activation", PBF[:, 0:jn], B.PSB[:, 0:jn], AF.Exp,
                    bias=st[:, 30:31])
                P.v("reduce_sum", st[:, 31:32], PBF[:, 0:jn], axis=AX.X)
                P.v("reciprocal", st[:, 32:33], st[:, 31:32])
                P.v("tensor_scalar_mul", B.SCB[:, 0:128], B.IDB[:],
                    st[:, 32:33])
                cb = jn // 128
                for c in range(cb):
                    P.mm(B.PS3[:, 0:128], PBF[:, c * 128:(c + 1) * 128],
                         B.SCB[:, 0:128], start=True, stop=True)
                    P.s("copy", PTB[:, c * 128:(c + 1) * 128], B.PS3[:, 0:128])
                for c in range(cb):
                    P.mm(B.PS1[0:64, 0:128],
                         VN[:, c * 128 + poff:c * 128 + poff + 64],
                         PTB[:, c * 128:(c + 1) * 128],
                         start=(c == 0), stop=(c == cb - 1))
                P.s("copy", OT2[poff:poff + 64, b * 128:(b + 1) * 128],
                    B.PS1[0:64, 0:128])
        gemmN_acc_group(P, B, A, lambda kc, tc: 3 * T + tc * 128, W["wo"],
                        [hp], acc_plain(P, B))
    bias_bc = None
    if "bo_bc" in W:
        P.dma(B.BBC[:], W["bo_bc"])
        bias_bc = B.BBC[:]
    residual_store(P, B, x_ap, y_ap, 1.0, bias_bc)



def ffn_l0(P, B, x_ap, y_ap, W, act):
    layer_norm_to_XT(P, B, x_ap)
    P.v("memset", B.YACC[:], 0.0)
    P.dma(B.BPP[:, 0:32], W["b1_pp"])
    for grp in range(4):
        gemm_T(P, B, W["w15"][grp * 8:(grp + 1) * 8], 8, DC, B.A, 0,
               act=act, bias_cols=grp * 8, bias_sb=B.BPP)
        gemmN_acc_group(P, B, B.A,
                        lambda kc, tc, g=grp: (kc - g * 8) * T + tc * 128,
                        W["w2"], list(range(grp * 8, (grp + 1) * 8)),
                        acc_plain(P, B))
    P.dma(B.BBC[:], W["b2_bc"])
    residual_store(P, B, x_ap, y_ap, 0.5, B.BBC[:])


def router_scores(P, B, router_ap, nE):
    for i in range(DC):
        P.dma(B.STGW[:, 0:nE], router_ap[i * 128:(i + 1) * 128, :])
        P.v("tensor_copy", B.WB[:, 7168 + i * 16:7168 + i * 16 + nE],
            B.STGW[:, 0:nE])
    for tc in range(TT):
        for kc in range(DC):
            P.mm(B.PS4[:, 0:nE],
                 B.XT[:, kc * T + tc * 128: kc * T + (tc + 1) * 128],
                 B.WB[:, 7168 + kc * 16:7168 + kc * 16 + nE],
                 start=(kc == 0), stop=(kc == DC - 1))
        topk2_row(P, B, B.PS4[:, 0:nE], B.SC[:, tc * 16:tc * 16 + nE], nE)


def ffn_l1(P, B, x_ap, y_ap, W):
    layer_norm_to_XT(P, B, x_ap)
    P.v("memset", B.YACC[:], 0.0)
    nE = 16
    router_scores(P, B, W["router"], nE)
    for e in range(nE):
        P.dma(B.BPP[:, 0:4], W["bg_pp"][e])
        P.dma(B.BPP[:, 8:12], W["bu_pp"][e])
        gemm_T(P, B, W["wg5"][e], 4, DC, B.A, 0, bias_cols=0, bias_sb=B.BPP)
        gemm_T(P, B, W["wu5"][e], 4, DC, B.A, 4 * T, bias_cols=8, bias_sb=B.BPP)
        G = B.A[:, 0:4 * T]
        U = B.A[:, 4 * T:8 * T]
        for hf in range(2):
            sl = slice(hf * 2 * T, hf * 2 * T + 2 * T)
            P.s("activation", B.WB[:, 0:2 * T], G[:, sl], AF.Sigmoid,
                scale=1.702)
            P.v("tensor_scalar_min", G[:, sl], G[:, sl], 7.0)
            P.v("tensor_mul", G[:, sl], G[:, sl], B.WB[:, 0:2 * T])
            P.v("tensor_scalar", U[:, sl], U[:, sl], -7.0, 7.0, ALU.max,
                ALU.min)
            P.v("scalar_tensor_tensor", U[:, sl], U[:, sl], 1.0, G[:, sl],
                ALU.add, ALU.mult)
        gemmN_acc_group(P, B, B.A, lambda kc, tc: 4 * T + kc * T + tc * 128,
                        W["wd"][e], list(range(4)),
                        acc_scaled_ap(P, B, lambda tc, e=e:
                                      B.SC[:, tc * 16 + e:tc * 16 + e + 1]))
    P.dma(B.S32F[0:16, 0:1024], W["db"])
    P.v("tensor_copy", B.S32[0:16, 0:1024], B.S32F[0:16, 0:1024])
    P.v("tensor_copy", B.SCB[:, :], B.SC[:, :])
    for tc in range(TT):
        P.mm(B.PS3[0:16, 0:128], B.SCB[:, tc * 16:(tc + 1) * 16], B.IDB[:],
             start=True, stop=True)
        P.s("copy", B.S32[0:16, 1024 + tc * 128:1024 + (tc + 1) * 128],
            B.PS3[0:16, 0:128])
    for tc in range(TT):
        for fc in range(2):
            P.mm(B.PS2[:, :],
                 B.S32[0:16, 1024 + tc * 128:1024 + (tc + 1) * 128],
                 B.S32[0:16, fc * 512:(fc + 1) * 512], start=True, stop=True)
            yc = B.YACC[:, tc * D + fc * 512: tc * D + (fc + 1) * 512]
            P.v("tensor_add", yc, yc, B.PS2[:, :])
    residual_store(P, B, x_ap, y_ap, 0.5, None)


def ffn_l2(P, B, x_ap, y_ap, W):
    layer_norm_to_XT(P, B, x_ap)
    P.v("memset", B.YACC[:], 0.0)
    st = B.STAT
    P.dma(st[0:1, 16:17], W["alpha"])
    P.v("tensor_scalar", st[0:1, 17:18], st[0:1, 16:17], 0.1, 1.0, ALU.max,
        ALU.min)
    P.v("tensor_scalar_mul", st[0:1, 18:19], st[0:1, 17:18], 0.5)
    P.v("tensor_scalar", st[0:1, 19:20], st[0:1, 17:18], -0.5, 0.5, ALU.mult,
        ALU.add)
    for dc in range(DC):
        P.v("reduce_sum", st[:, 26 + dc:27 + dc], B.XT[:, dc * T:(dc + 1) * T],
            axis=AX.X)
        P.s("activation", B.SCB[:, dc:dc + 1], st[:, 26 + dc:27 + dc],
            AF.Identity, scale=1.0 / T)
    for dc in range(DC):
        P.dma(B.STGW[:, 0:8], W["w_gate"][dc * 128:(dc + 1) * 128, :])
        P.v("tensor_copy", B.WB[:, 7168 + dc * 8:7168 + dc * 8 + 8],
            B.STGW[:, 0:8])
    for dc in range(DC):
        P.mm(B.PS4[0:1, 0:8], B.SCB[:, dc:dc + 1],
             B.WB[:, 7168 + dc * 8:7168 + dc * 8 + 8],
             start=(dc == 0), stop=(dc == DC - 1))
    P.dma(st[0:1, 40:48], W["b_gate"])
    P.v("tensor_add", st[0:1, 48:56], B.PS4[0:1, 0:8], st[0:1, 40:48])
    topk2_row(P, B, st[0:1, 48:56], st[0:1, 56:64], 8, prow=1)
    P.v("tensor_scalar_mul", st[0:1, 56:64], st[0:1, 56:64], st[0:1, 19:20])
    P.v("tensor_copy", B.SCB[0:1, 16:17], st[0:1, 18:19])
    P.v("tensor_copy", B.SCB[0:1, 17:25], st[0:1, 56:64])
    P.mm(B.PS4[:, 0:9], B.ONESB[:], B.SCB[0:1, 16:25], start=True, stop=True)
    P.v("tensor_copy", st[:, 0:9], B.PS4[:, 0:9])

    def one_mlp(w15, w25, b1_pp, b2_bc, col):
        P.dma(B.BPP[:, 0:32], b1_pp)
        for grp in range(4):
            gemm_T(P, B, w15[grp * 8:(grp + 1) * 8], 8, DC, B.A, 0,
                   act=AF.Gelu_apprx_tanh, bias_cols=grp * 8, bias_sb=B.BPP)
            gemmN_acc_group(P, B, B.A,
                            lambda kc, tc, g=grp: (kc - g * 8) * T + tc * 128,
                            w25, list(range(grp * 8, (grp + 1) * 8)),
                            acc_scaled_ap(P, B,
                                          lambda tc: st[:, col:col + 1]))
        P.dma(B.BBC[:], b2_bc)
        for tc in range(TT):
            for fc in range(2):
                P.v("tensor_scalar_mul", B.TMPV[:],
                    B.BBC[:, fc * 512:(fc + 1) * 512], st[:, col:col + 1])
                yc = B.YACC[:, tc * D + fc * 512: tc * D + (fc + 1) * 512]
                P.v("tensor_add", yc, yc, B.TMPV[:])

    one_mlp(W["w1m5"], W["w2m"], W["b1m_pp"], W["b2m_bc"], 0)
    for e in range(8):
        one_mlp(W["w1e5"][e], W["w2e"][e], W["b1e_pp"][e], W["b2e_bc"][e],
                1 + e)
    residual_store(P, B, x_ap, y_ap, 1.0, None)


def ffn_l3(P, B, x_ap, y_ap, W):
    layer_norm_to_XT(P, B, x_ap)
    P.v("memset", B.YACC[:], 0.0)
    nE = 16
    router_scores(P, B, W["router"], nE)
    for i in range(DC):
        P.dma(B.STGW[:, 0:1], W["wsg"][i * 128:(i + 1) * 128, :])
        P.v("tensor_copy", B.WB[:, 7168 + i:7168 + i + 1], B.STGW[:, 0:1])
    for tc in range(TT):
        for kc in range(DC):
            P.mm(B.PS4[:, 0:1],
                 B.XT[:, kc * T + tc * 128: kc * T + (tc + 1) * 128],
                 B.WB[:, 7168 + kc:7168 + kc + 1],
                 start=(kc == 0), stop=(kc == DC - 1))
        P.s("activation", B.STAT[:, 40 + tc:41 + tc], B.PS4[:, 0:1],
            AF.Sigmoid)
    for e in range(nE):
        gemm_T(P, B, W["wg5"][e], 4, DC, B.A, 0)
        gemm_T(P, B, W["wu5"][e], 4, DC, B.A, 4 * T)
        G = B.A[:, 0:4 * T]
        U = B.A[:, 4 * T:8 * T]
        for hf in range(2):
            sl = slice(hf * 2 * T, hf * 2 * T + 2 * T)
            P.s("activation", B.WB[:, 0:2 * T], G[:, sl], AF.Silu)
            P.v("tensor_mul", G[:, sl], B.WB[:, 0:2 * T], U[:, sl])
        gemmN_acc_group(P, B, B.A, lambda kc, tc: kc * T + tc * 128,
                        W["wd"][e], list(range(4)),
                        acc_scaled_ap(P, B, lambda tc, e=e:
                                      B.SC[:, tc * 16 + e:tc * 16 + e + 1]))
    for grp in range(4):
        gemm_T(P, B, W["wsg5"][grp * 4:(grp + 1) * 4], 4, DC, B.A, 0)
        gemm_T(P, B, W["wsu5"][grp * 4:(grp + 1) * 4], 4, DC, B.A, 4 * T)
        G = B.A[:, 0:4 * T]
        U = B.A[:, 4 * T:8 * T]
        for hf in range(2):
            sl = slice(hf * 2 * T, hf * 2 * T + 2 * T)
            P.s("activation", B.WB[:, 0:2 * T], G[:, sl], AF.Silu)
            P.v("tensor_mul", G[:, sl], B.WB[:, 0:2 * T], U[:, sl])
        gemmN_acc_group(P, B, B.A,
                        lambda kc, tc, g=grp: (kc - g * 4) * T + tc * 128,
                        W["wsd"], list(range(grp * 4, (grp + 1) * 4)),
                        acc_scaled_ap(P, B, lambda tc:
                                      B.STAT[:, 40 + tc:41 + tc]))
    residual_store(P, B, x_ap, y_ap, 0.5, None)



def tileT(w):
    K, M = w.shape
    assert M % 128 == 0
    return np.ascontiguousarray(
        w.reshape(K, M // 128, 128).transpose(1, 0, 2)).astype(np.float32)


def pp(b):
    n = b.shape[0] // 128
    return np.ascontiguousarray(b.reshape(n, 128).T).astype(np.float32)


def bc(b):
    return np.ascontiguousarray(
        np.broadcast_to(b.astype(np.float32), (128, b.shape[0])))


def host_consts():
    c = {}
    c["c_id"] = np.eye(128, dtype=BF)
    c["c_flip"] = np.ascontiguousarray(np.eye(128)[::-1]).astype(BF)
    mask = np.where(np.arange(128)[None, :] <= np.arange(128)[:, None],
                    0.0, NEG).astype(np.float32)
    maskf = mask[::-1, :]
    cm4 = np.zeros((128, 2048), dtype=np.float32)
    for v in range(4):
        cm4[:, v * 512 + v * 128: v * 512 + (v + 1) * 128] = maskf
    c["c_cm4"] = cm4.astype(BF)
    c["c_ones"] = np.ones((1, 128), dtype=BF)
    return c


def host_rope():
    i = np.arange(32, dtype=np.float32)
    inv = (1.0 / 10000.0 ** (2 * i / 64)).astype(np.float32)
    t = np.arange(T, dtype=np.float32)
    fr = np.outer(t, inv)
    emb = np.concatenate([fr, fr], axis=1)
    cos = np.cos(emb).T
    sin = np.sin(emb).T
    return (np.tile(cos, (2, 1)).astype(BF),
            np.tile(sin, (2, 1)).astype(BF))


def host_ohrevT():
    u = np.arange(4096)
    d = 2047 - u
    rp = np.maximum(d, 0)
    max_exact = 16
    is_small = rp < max_exact
    rl = max_exact + (np.log(rp.astype(np.float32) / max_exact + 1e-10)
                      / math.log(128 / max_exact)
                      * (32 - max_exact)).astype(np.int32)
    rl = np.minimum(rl, 31)
    bkt = np.where(is_small, rp, rl)
    oh = np.zeros((32, 4096), dtype=BF)
    oh[bkt, u] = 1.0
    return oh


def prep_layer0(inp):
    W = dict(host_consts())
    W["w15"] = tileT(inp["L0_ffn_w1"])
    W["b1_pp"] = pp(inp["L0_ffn_b1"])
    W["w2"] = np.ascontiguousarray(inp["L0_ffn_w2"]).astype(np.float32)
    W["b2_bc"] = bc(inp["L0_ffn_b2"])
    W["wq5"] = tileT(inp["L0_attn_Wq"])
    W["bq_pp"] = pp(inp["L0_attn_bq"])
    W["wl5"] = tileT(inp["L0_attn_Wl"])
    W["bl_pp"] = pp(inp["L0_attn_bl"])
    W["wk5"] = tileT(inp["L0_attn_Wk"])
    W["bk_pp"] = pp(inp["L0_attn_bk"])
    W["wv5"] = tileT(inp["L0_attn_Wv"])
    W["bv_pp"] = pp(inp["L0_attn_bv"])
    W["wo"] = np.ascontiguousarray(inp["L0_attn_Wo"]).astype(np.float32)
    W["bo_bc"] = bc(inp["L0_attn_bo"])
    return W


def prep_layer1(inp):
    W = dict(host_consts())
    cos, sin = host_rope()
    W["cosb"], W["sinb"] = cos, sin
    W["wq5"] = tileT(inp["L1_attn_Wq"].T)
    W["wk5"] = tileT(inp["L1_attn_Wk"].T)
    W["wv5"] = tileT(inp["L1_attn_Wv"].T)
    W["wo"] = np.ascontiguousarray(inp["L1_attn_Wo"].T).astype(np.float32)
    W["router"] = np.ascontiguousarray(
        inp["L1_ffn_router_weight"]).astype(np.float32)
    gup = inp["L1_ffn_gate_up_proj"]
    gub = inp["L1_ffn_gate_up_proj_bias"]
    W["wg5"] = np.stack([tileT(gup[e][:, 0::2]) for e in range(16)])
    W["wu5"] = np.stack([tileT(gup[e][:, 1::2]) for e in range(16)])
    W["bg_pp"] = np.stack([pp(gub[e][0::2]) for e in range(16)])
    W["bu_pp"] = np.stack([pp(gub[e][1::2]) for e in range(16)])
    W["wd"] = np.ascontiguousarray(inp["L1_ffn_down_proj"]).astype(np.float32)
    W["db"] = np.ascontiguousarray(
        inp["L1_ffn_down_proj_bias"]).astype(np.float32)
    return W


def prep_layer2(inp):
    W = dict(host_consts())
    waT = inp["L2_attn_W_attn"].T
    W["wq5"] = tileT(waT[:, 0:1024])
    W["wk5"] = tileT(waT[:, 1024:2048])
    W["wv5"] = tileT(waT[:, 2048:3072])
    ba = inp["L2_attn_b_attn"]
    W["bq_pp"] = pp(ba[0:1024])
    W["bk_pp"] = pp(ba[1024:2048])
    W["bv_pp"] = pp(ba[2048:3072])
    W["wo"] = np.ascontiguousarray(inp["L2_attn_W_proj"].T).astype(np.float32)
    W["bo_bc"] = bc(inp["L2_attn_b_proj"])
    W["rel_emb"] = np.ascontiguousarray(
        inp["L2_attn_rel_emb"]).astype(np.float32)
    W["ohrevT"] = host_ohrevT()
    W["alpha"] = np.asarray(inp["L2_ffn_alpha"],
                            dtype=np.float32).reshape(1, 1)
    W["w_gate"] = np.ascontiguousarray(inp["L2_ffn_w_gate"]).astype(np.float32)
    W["b_gate"] = np.ascontiguousarray(
        inp["L2_ffn_b_gate"]).astype(np.float32).reshape(1, 8)
    W["w1m5"] = tileT(inp["L2_ffn_W1m"])
    W["b1m_pp"] = pp(inp["L2_ffn_b1m"])
    W["w2m"] = np.ascontiguousarray(inp["L2_ffn_W2m"]).astype(np.float32)
    W["b2m_bc"] = bc(inp["L2_ffn_b2m"])
    W["w1e5"] = np.stack([tileT(inp["L2_ffn_W1e"][e]) for e in range(8)])
    W["b1e_pp"] = np.stack([pp(inp["L2_ffn_b1e"][e]) for e in range(8)])
    W["w2e"] = np.ascontiguousarray(inp["L2_ffn_W2e"]).astype(np.float32)
    W["b2e_bc"] = np.stack([bc(inp["L2_ffn_b2e"][e]) for e in range(8)])
    return W


def prep_layer3(inp):
    W = dict(host_consts())
    W["wq5"] = tileT(inp["L3_attn_WQ"].T)
    W["bq_pp"] = pp(inp["L3_attn_bQ"])
    W["wk5"] = tileT(inp["L3_attn_WK"].T)
    W["bk_pp"] = pp(inp["L3_attn_bK"])
    W["wv5"] = tileT(inp["L3_attn_WV"].T)
    W["bv_pp"] = pp(inp["L3_attn_bV"])
    W["wo"] = np.ascontiguousarray(inp["L3_attn_WO"].T).astype(np.float32)
    W["bo_bc"] = bc(inp["L3_attn_bO"])
    r3 = inp["L3_attn_rel_pos_table"]
    r3rev = np.ascontiguousarray(r3[::-1].T).astype(np.float32)
    W["r3rev"] = np.concatenate(
        [r3rev, np.zeros((16, 1), np.float32)], axis=1)
    W["router"] = np.ascontiguousarray(inp["L3_ffn_Wg"].T).astype(np.float32)
    W["wg5"] = np.stack(
        [tileT(inp["L3_ffn_We_gate"][e].T) for e in range(16)])
    W["wu5"] = np.stack([tileT(inp["L3_ffn_We_up"][e].T) for e in range(16)])
    W["wd"] = np.ascontiguousarray(
        inp["L3_ffn_We_down"].transpose(0, 2, 1)).astype(np.float32)
    W["wsg5"] = tileT(inp["L3_ffn_Ws_gate"].T)
    W["wsu5"] = tileT(inp["L3_ffn_Ws_up"].T)
    W["wsd"] = np.ascontiguousarray(inp["L3_ffn_Ws_down"].T).astype(np.float32)
    W["wsg"] = np.ascontiguousarray(
        inp["L3_ffn_Wsg"].T).astype(np.float32)
    return W



def declare_weights(P, Wnp, prefix):
    aps = {}
    for k, v in Wnp.items():
        dt = BF16 if v.dtype == BF else F32
        aps[k] = P.nc.dram_tensor(prefix + k, list(v.shape), dt,
                                  kind="ExternalInput").ap()
    return aps


def emit_layer(P, B, li, Waps, x_ap, y_ap, xa_ap, xb_ap):
    if li == 0:
        ffn_l0(P, B, x_ap, xa_ap, Waps, AF.Gelu)
        attention(P, B, xa_ap, xb_ap, Waps, "l0")
        ffn_l0(P, B, xb_ap, y_ap, Waps, AF.Gelu)
    elif li == 1:
        ffn_l1(P, B, x_ap, xa_ap, Waps)
        attention(P, B, xa_ap, xb_ap, Waps, "l1")
        ffn_l1(P, B, xb_ap, y_ap, Waps)
    elif li == 2:
        ffn_l2(P, B, x_ap, xa_ap, Waps)
        attention(P, B, xa_ap, xb_ap, Waps, "l2")
        ffn_l2(P, B, xb_ap, y_ap, Waps)
    else:
        ffn_l3(P, B, x_ap, xa_ap, Waps)
        attention(P, B, xa_ap, xb_ap, Waps, "l3")
        ffn_l3(P, B, xb_ap, y_ap, Waps)


def build_layer_program(li, Wnp):
    P = PB()
    B = Bufs(P)
    nc = P.nc
    x = nc.dram_tensor("x", [T, D], F32, kind="ExternalInput").ap()
    y = nc.dram_tensor("y", [T, D], F32, kind="ExternalOutput").ap()
    xa = nc.dram_tensor("xa", [T, D], F32, kind="Internal").ap()
    xb = nc.dram_tensor("xb", [T, D], F32, kind="Internal").ap()
    Waps = declare_weights(P, Wnp, "w_")
    if li in (2, 3):
        Waps["rtab"] = nc.dram_tensor("rtab", [16, 4096], BF16,
                                      kind="Internal").ap()
    load_consts(P, B, Waps)
    emit_layer(P, B, li, Waps, x, y, xa, xb)
    P.finish()
    return P


_PREPS = [prep_layer0, prep_layer1, prep_layer2, prep_layer3]

FUSED = False


def build_full_program(Ws):
    P = PB()
    B = Bufs(P)
    nc = P.nc
    chain = [nc.dram_tensor("x", [T, D], F32, kind="ExternalInput").ap()]
    for i in range(11):
        chain.append(nc.dram_tensor(f"xm{i}", [T, D], F32,
                                    kind="Internal").ap())
    chain.append(nc.dram_tensor("y", [T, D], F32, kind="ExternalOutput").ap())
    Waps_all = []
    for li, Wnp in enumerate(Ws):
        aps = {}
        for k, v in Wnp.items():
            if k.startswith("c_"):
                continue
            dt = BF16 if v.dtype == BF else F32
            aps[k] = nc.dram_tensor(f"L{li}_{k}", list(v.shape), dt,
                                    kind="ExternalInput").ap()
        if li in (2, 3):
            aps["rtab"] = nc.dram_tensor(f"rtab{li}", [16, 4096], BF16,
                                         kind="Internal").ap()
        Waps_all.append(aps)
    cst = Ws[0]
    caps = {}
    for k in ("c_id", "c_flip", "c_cm4", "c_ones"):
        caps[k] = nc.dram_tensor(k, list(cst[k].shape), BF16,
                                 kind="ExternalInput").ap()
    load_consts(P, B, caps)
    for li in range(4):
        Waps = dict(Waps_all[li])
        Waps.update(caps)
        emit_layer(P, B, li, Waps, chain[3 * li], chain[3 * li + 3],
                   chain[3 * li + 1], chain[3 * li + 2])
    P.finish()
    return P


def kernel(**inp):
    x = np.ascontiguousarray(inp["L0_attn_x"]).astype(np.float32)
    xs = [np.ascontiguousarray(x[c]) for c in range(8)]
    Ws = [_PREPS[li](inp) for li in range(4)]
    if FUSED:
        P = build_full_program(Ws)
        base = {}
        for li in range(4):
            for k, v in Ws[li].items():
                if not k.startswith("c_"):
                    base[f"L{li}_{k}"] = v
        for k in ("c_id", "c_flip", "c_cm4", "c_ones"):
            base[k] = Ws[0][k]
        in_maps = [dict(base, x=xs[c]) for c in range(8)]
        res = run_bass_kernel_spmd(P.nc, in_maps, core_ids=list(range(8)))
        out = np.stack([np.asarray(r["y"]) for r in res.results])
        P.st.close()
        return out.astype(np.float32)
    for li in range(4):
        Wnp = Ws[li]
        P = build_layer_program(li, Wnp)
        in_maps = []
        for c in range(8):
            m = {"w_" + k: v for k, v in Wnp.items()}
            m["x"] = xs[c]
            in_maps.append(m)
        res = run_bass_kernel_spmd(P.nc, in_maps, core_ids=list(range(8)))
        xs = [np.asarray(r["y"]) for r in res.results]
        P.st.close()
    return np.stack(xs).astype(np.float32)


if __name__ == "__main__":
    pass


# version 03 (final)
# speedup vs baseline: 1.0018x; 1.0018x over previous
import math
import contextlib
import numpy as np
import ml_dtypes
import concourse.bass as bass
import concourse.mybir as mybir
from concourse.bass_utils import run_bass_kernel_spmd

F32 = mybir.dt.float32
BF16 = mybir.dt.bfloat16
AF = mybir.ActivationFunctionType
ALU = mybir.AluOpType
AX = mybir.AxisListType
BF = ml_dtypes.bfloat16

T = 2048
D = 1024
TT = T // 128
DC = D // 128
NEG = -1.0e30



class PB:
    def __init__(self):
        nc = bass.Bass("TRN2", target_bir_lowering=False, debug=False)
        self.nc = nc
        self.st = contextlib.ExitStack()
        e = self.st.enter_context
        self.sem = {k: e(nc.semaphore("sem_" + k)) for k in ("t", "v", "s")}
        self.dsem = e(nc.semaphore("sem_d"))
        self.bsem = e(nc.semaphore("sem_b"))
        self.bepoch = 0
        self.cnt = {"t": 0, "v": 0, "s": 0}
        self.dcnt = 0
        self.lw = {}
        self.EN = {"t": nc.tensor, "v": nc.vector, "s": nc.scalar, "y": nc.sync}

    def sb(self, name, shape, dt):
        return self.st.enter_context(self.nc.sbuf_tensor(name, shape, dt))

    def ps(self, name, shape, dt):
        return self.st.enter_context(self.nc.psum_tensor(name, shape, dt))

    def _pre(self, ek):
        eng = self.EN[ek]
        for k in ("t", "v", "s"):
            if ek == k and ek == "t":
                continue
            tgt = self.cnt[k]
            if tgt > self.lw.get((ek, k), 0):
                eng.wait_ge(self.sem[k], tgt)
                self.lw[(ek, k)] = tgt
        if self.dcnt > self.lw.get((ek, "d"), 0):
            eng.wait_ge(self.dsem, 16 * self.dcnt)
            self.lw[(ek, "d")] = self.dcnt

    def op(self, ek, inst):
        inst.then_inc(self.sem[ek], 1)
        self.cnt[ek] += 1

    def mm(self, out, lhsT, rhs, start, stop):
        self._pre("t")
        self.op("t", self.nc.tensor.matmul(out, lhsT, rhs, start=start, stop=stop))

    def v(self, fname, *a, **kw):
        self._pre("v")
        self.op("v", getattr(self.nc.vector, fname)(*a, **kw))

    def s(self, fname, *a, **kw):
        self._pre("s")
        self.op("s", getattr(self.nc.scalar, fname)(*a, **kw))

    def dma(self, dst, src):
        self._pre("y")
        self.nc.sync.dma_start(dst, src).then_inc(self.dsem, 16)
        self.dcnt += 1

    def barrier_reset(self):
        base = 5 * self.bepoch
        for ek in ("t", "s", "y", "v"):
            self._pre(ek)
            self.EN[ek].sem_inc(self.bsem, 1)
        self.nc.vector.wait_ge(self.bsem, base + 4)
        for s in (self.sem["t"], self.sem["v"], self.sem["s"], self.dsem):
            self.nc.vector.sem_clear(s)
        self.nc.vector.sem_inc(self.bsem, 1)
        for ek in ("t", "s", "y"):
            self.EN[ek].wait_ge(self.bsem, base + 5)
        self.bepoch += 1
        self.cnt = {"t": 0, "v": 0, "s": 0}
        self.dcnt = 0
        self.lw = {}

    def finish(self):
        self._pre("y")
        if self.dcnt > self.lw.get(("y", "d"), 0):
            self.nc.sync.wait_ge(self.dsem, 16 * self.dcnt)
            self.lw[("y", "d")] = self.dcnt


def sliding_ap(dram2d, row, npart, ncols):
    a = dram2d[row:row + 1, 0:ncols].copy()
    a.ap[0] = [1, npart]
    return a



class Bufs:
    def __init__(self, P):
        self.YACC = P.sb("YACC", [128, TT * D], F32)
        self.XTILE = P.sb("XTILE", [128, D], F32)
        self.SQT = P.sb("SQT", [128, D], F32)
        self.XT = P.sb("XT", [128, DC * T], BF16)
        self.A = P.sb("A", [128, 8 * T], BF16)
        self.WB = P.sb("WB", [128, 8192], BF16)
        self.STAT = P.sb("STAT", [128, 64], F32)
        self.SC = P.sb("SC", [128, TT * 16], F32)
        self.SCB = P.sb("SCB", [128, TT * 16], BF16)
        self.BPP = P.sb("BPP", [128, 64], F32)
        self.BPP2 = P.sb("BPP2", [128, 64], F32)
        self.BBC = P.sb("BBC", [128, D], F32)
        self.TMPV = P.sb("TMPV", [128, 512], F32)
        self.IDB = P.sb("IDB", [128, 128], BF16)
        self.FLIPB = P.sb("FLIPB", [128, 128], BF16)
        self.CM4 = P.sb("CM4", [128, 2048], BF16)
        self.ONESB = P.sb("ONESB", [1, 128], BF16)
        self.S32 = P.sb("S32", [32, 4096], BF16)
        self.S32F = P.sb("S32F", [32, 2048], F32)
        self.STG = P.sb("STG", [128, 1024], F32)
        self.STGW = P.sb("STGW", [128, 1024], F32)
        self.PSB = P.ps("PSB", [128, 2048], F32)
        self.PS1 = P.ps("PS1", [128, 512], F32)
        self.PS2 = P.ps("PS2", [128, 512], F32)
        self.PS3 = P.ps("PS3", [128, 512], F32)
        self.PS4 = P.ps("PS4", [128, 512], F32)


def load_consts(P, B, W):
    P.dma(B.IDB[:], W["c_id"])
    P.dma(B.FLIPB[:], W["c_flip"])
    P.dma(B.CM4[:], W["c_cm4"])
    P.dma(B.ONESB[:], W["c_ones"])



def layer_norm_to_XT(P, B, x_ap):
    xnb = B.A[:, 7 * T:7 * T + D]
    for tc in range(TT):
        P.dma(B.XTILE[:], x_ap[tc * 128:(tc + 1) * 128, :])
        st = B.STAT
        P.v("reduce_sum", st[:, 0:1], B.XTILE[:], axis=AX.X)
        P.s("activation", B.SQT[:], B.XTILE[:], AF.Square)
        P.v("reduce_sum", st[:, 1:2], B.SQT[:], axis=AX.X)
        P.v("tensor_scalar_mul", st[:, 2:3], st[:, 0:1], 1.0 / D)
        P.v("tensor_scalar_mul", st[:, 3:4], st[:, 1:2], 1.0 / D)
        P.v("tensor_mul", st[:, 4:5], st[:, 2:3], st[:, 2:3])
        P.v("tensor_sub", st[:, 5:6], st[:, 3:4], st[:, 4:5])
        P.v("tensor_scalar_add", st[:, 5:6], st[:, 5:6], 1e-5)
        P.s("sqrt", st[:, 6:7], st[:, 5:6])
        P.v("reciprocal", st[:, 7:8], st[:, 6:7])
        P.v("tensor_mul", st[:, 8:9], st[:, 2:3], st[:, 7:8])
        P.v("tensor_scalar_mul", st[:, 9:10], st[:, 8:9], -1.0)
        P.s("activation", xnb, B.XTILE[:], AF.Identity, bias=st[:, 9:10],
            scale=st[:, 7:8])
        for dc in range(DC):
            P.mm(B.PS3[:, 0:128], xnb[:, dc * 128:(dc + 1) * 128], B.IDB[:],
                 start=True, stop=True)
            P.s("copy", B.XT[:, dc * T + tc * 128: dc * T + (tc + 1) * 128],
                B.PS3[:, 0:128])


def stage_wT(P, B, w5_ap, mc, kc_n):
    src = w5_ap[mc].rearrange("(kc p) c -> p kc c", p=128)
    dst = B.WB[:, 0:kc_n * 128].rearrange("p (kc c) -> p kc c", c=128)
    P.dma(B.STG[:, 0:kc_n * 128].rearrange("p (kc c) -> p kc c", c=128), src)
    P.v("tensor_copy", dst, B.STG[:, 0:kc_n * 128])


def gemm_T(P, B, w5_ap, m_chunks, kc_n, out_sb, out_col, act=AF.Identity,
           bias_cols=None, bias_sb=None, scale=1.0, rhs_sb=None, rhs_col0=0):
    if rhs_sb is None:
        rhs_sb, rhs_col0 = B.XT, 0
    for mc in range(m_chunks):
        stage_wT(P, B, w5_ap, mc, kc_n)
        for t4 in range(4):
            for kc in range(kc_n):
                P.mm(B.PSB[:, t4 * 512:(t4 + 1) * 512],
                     B.WB[:, kc * 128:(kc + 1) * 128],
                     rhs_sb[:, rhs_col0 + kc * T + t4 * 512:
                            rhs_col0 + kc * T + (t4 + 1) * 512],
                     start=(kc == 0), stop=(kc == kc_n - 1))
            kw = {}
            if bias_sb is not None:
                kw["bias"] = bias_sb[:, bias_cols + mc: bias_cols + mc + 1]
            if scale != 1.0:
                kw["scale"] = scale
            P.s("activation",
                out_sb[:, out_col + mc * T + t4 * 512:
                       out_col + mc * T + (t4 + 1) * 512],
                B.PSB[:, t4 * 512:(t4 + 1) * 512], act, **kw)


def gemmN_acc_group(P, B, lhsT_sb, lcol_fn, w_dram, krange, post):
    n = len(krange)
    for i, kc in enumerate(krange):
        P.dma(B.STGW[:], w_dram[kc * 128:(kc + 1) * 128, :])
        P.v("tensor_copy", B.WB[:, i * 1024:(i + 1) * 1024], B.STGW[:])
    for tc in range(TT):
        for fc in range(2):
            for i, kc in enumerate(krange):
                P.mm(B.PS2[:, :],
                     lhsT_sb[:, lcol_fn(kc, tc):lcol_fn(kc, tc) + 128],
                     B.WB[:, i * 1024 + fc * 512: i * 1024 + (fc + 1) * 512],
                     start=(i == 0), stop=(i == n - 1))
            post(tc, fc, B.PS2[:, :])


def acc_plain(P, B):
    def post(tc, fc, psum):
        yc = B.YACC[:, tc * D + fc * 512: tc * D + (fc + 1) * 512]
        P.v("tensor_add", yc, yc, psum)
    return post


def acc_scaled_ap(P, B, scale_ap_fn):
    def post(tc, fc, psum):
        yc = B.YACC[:, tc * D + fc * 512: tc * D + (fc + 1) * 512]
        P.v("tensor_scalar_mul", B.TMPV[:], psum, scale_ap_fn(tc))
        P.v("tensor_add", yc, yc, B.TMPV[:])
    return post


def residual_store(P, B, x_ap, y_ap, scale, bias_bc=None):
    for tc in range(TT):
        P.dma(B.XTILE[:], x_ap[tc * 128:(tc + 1) * 128, :])
        ya = B.YACC[:, tc * D:(tc + 1) * D]
        if bias_bc is not None:
            P.v("tensor_add", B.SQT[:], ya, bias_bc)
            P.v("scalar_tensor_tensor", B.SQT[:], B.SQT[:], float(scale),
                B.XTILE[:], ALU.mult, ALU.add)
        else:
            P.v("scalar_tensor_tensor", B.SQT[:], ya, float(scale),
                B.XTILE[:], ALU.mult, ALU.add)
        P.dma(y_ap[tc * 128:(tc + 1) * 128, :], B.SQT[:])


def topk2_row(P, B, src_ap, dst_ap, nE, prow=128):
    st = B.STAT
    t = B.TMPV
    P.v("tensor_copy", t[0:prow, 0:nE], src_ap)
    P.v("reduce_max", st[0:prow, 20:21], t[0:prow, 0:nE], axis=AX.X, negate=True)
    P.v("tensor_scalar_mul", st[0:prow, 21:22], st[0:prow, 20:21], -1.0)
    P.v("tensor_scalar", t[0:prow, 32:32 + nE], t[0:prow, 0:nE],
        st[0:prow, 21:22], None, ALU.is_ge)
    P.v("scalar_tensor_tensor", t[0:prow, 64:64 + nE], t[0:prow, 32:32 + nE],
        NEG, t[0:prow, 0:nE], ALU.mult, ALU.add)
    P.v("reduce_max", st[0:prow, 22:23], t[0:prow, 64:64 + nE], axis=AX.X)
    P.s("activation", t[0:prow, 96:96 + nE], t[0:prow, 0:nE], AF.Exp,
        bias=st[0:prow, 20:21])
    P.v("tensor_scalar", t[0:prow, 128:128 + nE], t[0:prow, 0:nE],
        st[0:prow, 22:23], None, ALU.is_ge)
    P.v("tensor_mul", t[0:prow, 160:160 + nE], t[0:prow, 96:96 + nE],
        t[0:prow, 128:128 + nE])
    P.v("reduce_sum", st[0:prow, 23:24], t[0:prow, 160:160 + nE], axis=AX.X)
    P.v("reciprocal", st[0:prow, 24:25], st[0:prow, 23:24])
    P.v("tensor_scalar_mul", dst_ap, t[0:prow, 160:160 + nE],
        st[0:prow, 24:25])


def rope_apply(P, B, cos_sb, sin_sb, qt, scale):
    rot = B.A[:, 4 * T:5 * T]
    if scale != 1.0:
        P.v("tensor_scalar_mul", qt, qt, float(scale))
    P.s("mul", rot[0:32, :], qt[32:64, :], -1.0)
    P.s("copy", rot[32:64, :], qt[0:32, :])
    P.s("mul", rot[64:96, :], qt[96:128, :], -1.0)
    P.s("copy", rot[96:128, :], qt[64:96, :])
    P.v("tensor_mul", qt, qt, cos_sb)
    P.v("tensor_mul", rot[:, :], rot[:, :], sin_sb)
    P.v("tensor_add", qt, qt, rot[:, :])



def attention(P, B, x_ap, y_ap, W, kind):
    A = B.A
    QT2 = A[:, 0 * T:1 * T]
    KT2 = A[:, 1 * T:2 * T]
    VN = A[:, 2 * T:3 * T]
    OT2 = A[:, 3 * T:4 * T]
    PBF = A[:, 4 * T:5 * T]
    PTB = A[:, 5 * T:6 * T]
    SLD = A[:, 6 * T:7 * T]
    VT2 = A[:, 7 * T:8 * T]
    causal = kind != "l0"
    scale = 0.125

    layer_norm_to_XT(P, B, x_ap)
    P.v("memset", B.YACC[:], 0.0)

    if kind == "l2":
        P.dma(B.S32F[0:32, 0:16], W["rel_emb"])
        P.v("tensor_copy", B.S32[0:32, 0:16], B.S32F[0:32, 0:16])
        P.dma(A[0:32, 0:4096], W["ohrevT"])
        for fc in range(8):
            P.mm(B.PS4[0:16, :], B.S32[0:32, 0:16],
                 A[0:32, fc * 512:(fc + 1) * 512], start=True, stop=True)
            P.s("copy", B.S32[0:16, fc * 512:(fc + 1) * 512], B.PS4[0:16, :])
        P.dma(W["rtab"][:, :], B.S32[0:16, :])
    if kind == "l3":
        for hf in range(2):
            P.dma(B.S32F[0:16, :], W["r3rev"][:, hf * 2048:(hf + 1) * 2048])
            P.v("tensor_copy", B.S32[0:16, hf * 2048:(hf + 1) * 2048],
                B.S32F[0:16, :])
        P.dma(W["rtab"][:, :], B.S32[0:16, :])

    if kind == "l0":
        P.dma(B.BPP[:, 32:34], W["bl_pp"])
        gemm_T(P, B, W["wl5"], 2, DC, B.WB, 4096, act=AF.Identity,
               bias_cols=32, bias_sb=B.BPP)
    if kind != "l1":
        P.dma(B.BPP[:, 0:DC], W["bv_pp"])
        P.dma(B.BPP[:, 8:8 + DC], W["bk_pp"])
        P.dma(B.BPP2[:, 0:DC], W["bq_pp"])
        P.v("tensor_scalar_mul", B.BPP[:, 16:16 + DC], B.BPP2[:, 0:DC], scale)
    if kind == "l1":
        P.dma(B.WB[:, 4096:4096 + T], W["cosb"])
        P.dma(B.WB[:, 4096 + T:4096 + 2 * T], W["sinb"])

    for hp in range(8):
        if kind == "l1":
            gemm_T(P, B, W["wq5"][hp:hp + 1], 1, DC, A, 0 * T)
            rope_apply(P, B, B.WB[:, 4096:4096 + T],
                       B.WB[:, 4096 + T:4096 + 2 * T], QT2[:, :], scale)
            gemm_T(P, B, W["wk5"][hp:hp + 1], 1, DC, A, 1 * T)
            rope_apply(P, B, B.WB[:, 4096:4096 + T],
                       B.WB[:, 4096 + T:4096 + 2 * T], KT2[:, :], 1.0)
            gemm_T(P, B, W["wv5"][hp:hp + 1], 1, DC, A, 7 * T)
        elif kind == "l0":
            gemm_T(P, B, W["wq5"][hp:hp + 1], 1, DC, A, 0 * T,
                   bias_cols=16 + hp, bias_sb=B.BPP, scale=scale)
            gemm_T(P, B, W["wk5"][hp:hp + 1], 1, 2, A, 1 * T,
                   bias_cols=8 + hp, bias_sb=B.BPP, rhs_sb=B.WB, rhs_col0=4096)
            gemm_T(P, B, W["wv5"][hp:hp + 1], 1, 2, A, 7 * T,
                   bias_cols=0 + hp, bias_sb=B.BPP, rhs_sb=B.WB, rhs_col0=4096)
        else:
            gemm_T(P, B, W["wq5"][hp:hp + 1], 1, DC, A, 0 * T,
                   bias_cols=16 + hp, bias_sb=B.BPP, scale=scale)
            gemm_T(P, B, W["wk5"][hp:hp + 1], 1, DC, A, 1 * T,
                   bias_cols=8 + hp, bias_sb=B.BPP)
            gemm_T(P, B, W["wv5"][hp:hp + 1], 1, DC, A, 7 * T,
                   bias_cols=0 + hp, bias_sb=B.BPP)
        for tc in range(TT):
            P.mm(B.PS3[:, 0:128], VT2[:, tc * 128:(tc + 1) * 128], B.IDB[:],
                 start=True, stop=True)
            P.s("copy", VN[:, tc * 128:(tc + 1) * 128], B.PS3[:, 0:128])

        for h2 in range(2):
            h = hp * 2 + h2
            poff = h2 * 64
            if kind in ("l2", "l3"):
                P.dma(SLD[:, :], sliding_ap(W["rtab"], h, 128, 2048))
            for b in range(TT):
                jn = (b + 1) * 128 if causal else T
                nch = (jn + 511) // 512
                for jc in range(nch):
                    j0 = jc * 512
                    ln = min(512, jn - j0)
                    pslice = B.PSB[:, j0:j0 + ln]
                    first = True
                    if causal and jc == nch - 1:
                        v = ln // 128 - 1
                        P.mm(pslice, B.FLIPB[:],
                             B.CM4[:, v * 512:v * 512 + ln],
                             start=True, stop=False)
                        first = False
                    if kind in ("l2", "l3"):
                        w0 = 1920 - 128 * b + j0
                        P.mm(pslice, B.FLIPB[:], SLD[:, w0:w0 + ln],
                             start=first, stop=False)
                        first = False
                    P.mm(pslice, QT2[poff:poff + 64, b * 128:(b + 1) * 128],
                         KT2[poff:poff + 64, j0:j0 + ln],
                         start=first, stop=True)
                st = B.STAT
                P.v("reduce_max", st[:, 30:31], B.PSB[:, 0:jn], axis=AX.X,
                    negate=True)
                P.s("activation", PBF[:, 0:jn], B.PSB[:, 0:jn], AF.Exp,
                    bias=st[:, 30:31])
                P.v("reduce_sum", st[:, 31:32], PBF[:, 0:jn], axis=AX.X)
                P.v("reciprocal", st[:, 32:33], st[:, 31:32])
                P.v("tensor_scalar_mul", B.SCB[:, 0:128], B.IDB[:],
                    st[:, 32:33])
                cb = jn // 128
                for c in range(cb):
                    P.mm(B.PS3[:, 0:128], PBF[:, c * 128:(c + 1) * 128],
                         B.SCB[:, 0:128], start=True, stop=True)
                    P.s("copy", PTB[:, c * 128:(c + 1) * 128], B.PS3[:, 0:128])
                for c in range(cb):
                    P.mm(B.PS1[0:64, 0:128],
                         VN[:, c * 128 + poff:c * 128 + poff + 64],
                         PTB[:, c * 128:(c + 1) * 128],
                         start=(c == 0), stop=(c == cb - 1))
                P.s("copy", OT2[poff:poff + 64, b * 128:(b + 1) * 128],
                    B.PS1[0:64, 0:128])
        gemmN_acc_group(P, B, A, lambda kc, tc: 3 * T + tc * 128, W["wo"],
                        [hp], acc_plain(P, B))
    bias_bc = None
    if "bo_bc" in W:
        P.dma(B.BBC[:], W["bo_bc"])
        bias_bc = B.BBC[:]
    residual_store(P, B, x_ap, y_ap, 1.0, bias_bc)



def ffn_l0(P, B, x_ap, y_ap, W, act):
    layer_norm_to_XT(P, B, x_ap)
    P.v("memset", B.YACC[:], 0.0)
    P.dma(B.BPP[:, 0:32], W["b1_pp"])
    for grp in range(4):
        gemm_T(P, B, W["w15"][grp * 8:(grp + 1) * 8], 8, DC, B.A, 0,
               act=act, bias_cols=grp * 8, bias_sb=B.BPP)
        gemmN_acc_group(P, B, B.A,
                        lambda kc, tc, g=grp: (kc - g * 8) * T + tc * 128,
                        W["w2"], list(range(grp * 8, (grp + 1) * 8)),
                        acc_plain(P, B))
    P.dma(B.BBC[:], W["b2_bc"])
    residual_store(P, B, x_ap, y_ap, 0.5, B.BBC[:])


def router_scores(P, B, router_ap, nE):
    for i in range(DC):
        P.dma(B.STGW[:, 0:nE], router_ap[i * 128:(i + 1) * 128, :])
        P.v("tensor_copy", B.WB[:, 7168 + i * 16:7168 + i * 16 + nE],
            B.STGW[:, 0:nE])
    for tc in range(TT):
        for kc in range(DC):
            P.mm(B.PS4[:, 0:nE],
                 B.XT[:, kc * T + tc * 128: kc * T + (tc + 1) * 128],
                 B.WB[:, 7168 + kc * 16:7168 + kc * 16 + nE],
                 start=(kc == 0), stop=(kc == DC - 1))
        topk2_row(P, B, B.PS4[:, 0:nE], B.SC[:, tc * 16:tc * 16 + nE], nE)


def ffn_l1(P, B, x_ap, y_ap, W):
    layer_norm_to_XT(P, B, x_ap)
    P.v("memset", B.YACC[:], 0.0)
    nE = 16
    router_scores(P, B, W["router"], nE)
    for e in range(nE):
        P.dma(B.BPP[:, 0:4], W["bg_pp"][e])
        P.dma(B.BPP[:, 8:12], W["bu_pp"][e])
        gemm_T(P, B, W["wg5"][e], 4, DC, B.A, 0, bias_cols=0, bias_sb=B.BPP)
        gemm_T(P, B, W["wu5"][e], 4, DC, B.A, 4 * T, bias_cols=8, bias_sb=B.BPP)
        G = B.A[:, 0:4 * T]
        U = B.A[:, 4 * T:8 * T]
        for hf in range(2):
            sl = slice(hf * 2 * T, hf * 2 * T + 2 * T)
            P.s("activation", B.WB[:, 0:2 * T], G[:, sl], AF.Sigmoid,
                scale=1.702)
            P.v("tensor_scalar_min", G[:, sl], G[:, sl], 7.0)
            P.v("tensor_mul", G[:, sl], G[:, sl], B.WB[:, 0:2 * T])
            P.v("tensor_scalar", U[:, sl], U[:, sl], -7.0, 7.0, ALU.max,
                ALU.min)
            P.v("scalar_tensor_tensor", U[:, sl], U[:, sl], 1.0, G[:, sl],
                ALU.add, ALU.mult)
        gemmN_acc_group(P, B, B.A, lambda kc, tc: 4 * T + kc * T + tc * 128,
                        W["wd"][e], list(range(4)),
                        acc_scaled_ap(P, B, lambda tc, e=e:
                                      B.SC[:, tc * 16 + e:tc * 16 + e + 1]))
    P.dma(B.S32F[0:16, 0:1024], W["db"])
    P.v("tensor_copy", B.S32[0:16, 0:1024], B.S32F[0:16, 0:1024])
    P.v("tensor_copy", B.SCB[:, :], B.SC[:, :])
    for tc in range(TT):
        P.mm(B.PS3[0:16, 0:128], B.SCB[:, tc * 16:(tc + 1) * 16], B.IDB[:],
             start=True, stop=True)
        P.s("copy", B.S32[0:16, 1024 + tc * 128:1024 + (tc + 1) * 128],
            B.PS3[0:16, 0:128])
    for tc in range(TT):
        for fc in range(2):
            P.mm(B.PS2[:, :],
                 B.S32[0:16, 1024 + tc * 128:1024 + (tc + 1) * 128],
                 B.S32[0:16, fc * 512:(fc + 1) * 512], start=True, stop=True)
            yc = B.YACC[:, tc * D + fc * 512: tc * D + (fc + 1) * 512]
            P.v("tensor_add", yc, yc, B.PS2[:, :])
    residual_store(P, B, x_ap, y_ap, 0.5, None)


def ffn_l2(P, B, x_ap, y_ap, W):
    layer_norm_to_XT(P, B, x_ap)
    P.v("memset", B.YACC[:], 0.0)
    st = B.STAT
    P.dma(st[0:1, 16:17], W["alpha"])
    P.v("tensor_scalar", st[0:1, 17:18], st[0:1, 16:17], 0.1, 1.0, ALU.max,
        ALU.min)
    P.v("tensor_scalar_mul", st[0:1, 18:19], st[0:1, 17:18], 0.5)
    P.v("tensor_scalar", st[0:1, 19:20], st[0:1, 17:18], -0.5, 0.5, ALU.mult,
        ALU.add)
    for dc in range(DC):
        P.v("reduce_sum", st[:, 26 + dc:27 + dc], B.XT[:, dc * T:(dc + 1) * T],
            axis=AX.X)
        P.s("activation", B.SCB[:, dc:dc + 1], st[:, 26 + dc:27 + dc],
            AF.Identity, scale=1.0 / T)
    for dc in range(DC):
        P.dma(B.STGW[:, 0:8], W["w_gate"][dc * 128:(dc + 1) * 128, :])
        P.v("tensor_copy", B.WB[:, 7168 + dc * 8:7168 + dc * 8 + 8],
            B.STGW[:, 0:8])
    for dc in range(DC):
        P.mm(B.PS4[0:1, 0:8], B.SCB[:, dc:dc + 1],
             B.WB[:, 7168 + dc * 8:7168 + dc * 8 + 8],
             start=(dc == 0), stop=(dc == DC - 1))
    P.dma(st[0:1, 40:48], W["b_gate"])
    P.v("tensor_add", st[0:1, 48:56], B.PS4[0:1, 0:8], st[0:1, 40:48])
    topk2_row(P, B, st[0:1, 48:56], st[0:1, 56:64], 8, prow=1)
    P.v("tensor_scalar_mul", st[0:1, 56:64], st[0:1, 56:64], st[0:1, 19:20])
    P.v("tensor_copy", B.SCB[0:1, 16:17], st[0:1, 18:19])
    P.v("tensor_copy", B.SCB[0:1, 17:25], st[0:1, 56:64])
    P.mm(B.PS4[:, 0:9], B.ONESB[:], B.SCB[0:1, 16:25], start=True, stop=True)
    P.v("tensor_copy", st[:, 0:9], B.PS4[:, 0:9])

    def one_mlp(w15, w25, b1_pp, b2_bc, col):
        P.dma(B.BPP[:, 0:32], b1_pp)
        for grp in range(4):
            gemm_T(P, B, w15[grp * 8:(grp + 1) * 8], 8, DC, B.A, 0,
                   act=AF.Gelu_apprx_tanh, bias_cols=grp * 8, bias_sb=B.BPP)
            gemmN_acc_group(P, B, B.A,
                            lambda kc, tc, g=grp: (kc - g * 8) * T + tc * 128,
                            w25, list(range(grp * 8, (grp + 1) * 8)),
                            acc_scaled_ap(P, B,
                                          lambda tc: st[:, col:col + 1]))
        P.dma(B.BBC[:], b2_bc)
        for tc in range(TT):
            for fc in range(2):
                P.v("tensor_scalar_mul", B.TMPV[:],
                    B.BBC[:, fc * 512:(fc + 1) * 512], st[:, col:col + 1])
                yc = B.YACC[:, tc * D + fc * 512: tc * D + (fc + 1) * 512]
                P.v("tensor_add", yc, yc, B.TMPV[:])

    one_mlp(W["w1m5"], W["w2m"], W["b1m_pp"], W["b2m_bc"], 0)
    for e in range(8):
        one_mlp(W["w1e5"][e], W["w2e"][e], W["b1e_pp"][e], W["b2e_bc"][e],
                1 + e)
    residual_store(P, B, x_ap, y_ap, 1.0, None)


def ffn_l3(P, B, x_ap, y_ap, W):
    layer_norm_to_XT(P, B, x_ap)
    P.v("memset", B.YACC[:], 0.0)
    nE = 16
    router_scores(P, B, W["router"], nE)
    for i in range(DC):
        P.dma(B.STGW[:, 0:1], W["wsg"][i * 128:(i + 1) * 128, :])
        P.v("tensor_copy", B.WB[:, 7168 + i:7168 + i + 1], B.STGW[:, 0:1])
    for tc in range(TT):
        for kc in range(DC):
            P.mm(B.PS4[:, 0:1],
                 B.XT[:, kc * T + tc * 128: kc * T + (tc + 1) * 128],
                 B.WB[:, 7168 + kc:7168 + kc + 1],
                 start=(kc == 0), stop=(kc == DC - 1))
        P.s("activation", B.STAT[:, 40 + tc:41 + tc], B.PS4[:, 0:1],
            AF.Sigmoid)
    for e in range(nE):
        gemm_T(P, B, W["wg5"][e], 4, DC, B.A, 0)
        gemm_T(P, B, W["wu5"][e], 4, DC, B.A, 4 * T)
        G = B.A[:, 0:4 * T]
        U = B.A[:, 4 * T:8 * T]
        for hf in range(2):
            sl = slice(hf * 2 * T, hf * 2 * T + 2 * T)
            P.s("activation", B.WB[:, 0:2 * T], G[:, sl], AF.Silu)
            P.v("tensor_mul", G[:, sl], B.WB[:, 0:2 * T], U[:, sl])
        gemmN_acc_group(P, B, B.A, lambda kc, tc: kc * T + tc * 128,
                        W["wd"][e], list(range(4)),
                        acc_scaled_ap(P, B, lambda tc, e=e:
                                      B.SC[:, tc * 16 + e:tc * 16 + e + 1]))
    for grp in range(4):
        gemm_T(P, B, W["wsg5"][grp * 4:(grp + 1) * 4], 4, DC, B.A, 0)
        gemm_T(P, B, W["wsu5"][grp * 4:(grp + 1) * 4], 4, DC, B.A, 4 * T)
        G = B.A[:, 0:4 * T]
        U = B.A[:, 4 * T:8 * T]
        for hf in range(2):
            sl = slice(hf * 2 * T, hf * 2 * T + 2 * T)
            P.s("activation", B.WB[:, 0:2 * T], G[:, sl], AF.Silu)
            P.v("tensor_mul", G[:, sl], B.WB[:, 0:2 * T], U[:, sl])
        gemmN_acc_group(P, B, B.A,
                        lambda kc, tc, g=grp: (kc - g * 4) * T + tc * 128,
                        W["wsd"], list(range(grp * 4, (grp + 1) * 4)),
                        acc_scaled_ap(P, B, lambda tc:
                                      B.STAT[:, 40 + tc:41 + tc]))
    residual_store(P, B, x_ap, y_ap, 0.5, None)



def tileT(w):
    K, M = w.shape
    assert M % 128 == 0
    return np.ascontiguousarray(
        w.reshape(K, M // 128, 128).transpose(1, 0, 2)).astype(np.float32)


def pp(b):
    n = b.shape[0] // 128
    return np.ascontiguousarray(b.reshape(n, 128).T).astype(np.float32)


def bc(b):
    return np.ascontiguousarray(
        np.broadcast_to(b.astype(np.float32), (128, b.shape[0])))


def host_consts():
    c = {}
    c["c_id"] = np.eye(128, dtype=BF)
    c["c_flip"] = np.ascontiguousarray(np.eye(128)[::-1]).astype(BF)
    mask = np.where(np.arange(128)[None, :] <= np.arange(128)[:, None],
                    0.0, NEG).astype(np.float32)
    maskf = mask[::-1, :]
    cm4 = np.zeros((128, 2048), dtype=np.float32)
    for v in range(4):
        cm4[:, v * 512 + v * 128: v * 512 + (v + 1) * 128] = maskf
    c["c_cm4"] = cm4.astype(BF)
    c["c_ones"] = np.ones((1, 128), dtype=BF)
    return c


def host_rope():
    i = np.arange(32, dtype=np.float32)
    inv = (1.0 / 10000.0 ** (2 * i / 64)).astype(np.float32)
    t = np.arange(T, dtype=np.float32)
    fr = np.outer(t, inv)
    emb = np.concatenate([fr, fr], axis=1)
    cos = np.cos(emb).T
    sin = np.sin(emb).T
    return (np.tile(cos, (2, 1)).astype(BF),
            np.tile(sin, (2, 1)).astype(BF))


def host_ohrevT():
    u = np.arange(4096)
    d = 2047 - u
    rp = np.maximum(d, 0)
    max_exact = 16
    is_small = rp < max_exact
    rl = max_exact + (np.log(rp.astype(np.float32) / max_exact + 1e-10)
                      / math.log(128 / max_exact)
                      * (32 - max_exact)).astype(np.int32)
    rl = np.minimum(rl, 31)
    bkt = np.where(is_small, rp, rl)
    oh = np.zeros((32, 4096), dtype=BF)
    oh[bkt, u] = 1.0
    return oh


def prep_layer0(inp):
    W = dict(host_consts())
    W["w15"] = tileT(inp["L0_ffn_w1"])
    W["b1_pp"] = pp(inp["L0_ffn_b1"])
    W["w2"] = np.ascontiguousarray(inp["L0_ffn_w2"]).astype(np.float32)
    W["b2_bc"] = bc(inp["L0_ffn_b2"])
    W["wq5"] = tileT(inp["L0_attn_Wq"])
    W["bq_pp"] = pp(inp["L0_attn_bq"])
    W["wl5"] = tileT(inp["L0_attn_Wl"])
    W["bl_pp"] = pp(inp["L0_attn_bl"])
    W["wk5"] = tileT(inp["L0_attn_Wk"])
    W["bk_pp"] = pp(inp["L0_attn_bk"])
    W["wv5"] = tileT(inp["L0_attn_Wv"])
    W["bv_pp"] = pp(inp["L0_attn_bv"])
    W["wo"] = np.ascontiguousarray(inp["L0_attn_Wo"]).astype(np.float32)
    W["bo_bc"] = bc(inp["L0_attn_bo"])
    return W


def prep_layer1(inp):
    W = dict(host_consts())
    cos, sin = host_rope()
    W["cosb"], W["sinb"] = cos, sin
    W["wq5"] = tileT(inp["L1_attn_Wq"].T)
    W["wk5"] = tileT(inp["L1_attn_Wk"].T)
    W["wv5"] = tileT(inp["L1_attn_Wv"].T)
    W["wo"] = np.ascontiguousarray(inp["L1_attn_Wo"].T).astype(np.float32)
    W["router"] = np.ascontiguousarray(
        inp["L1_ffn_router_weight"]).astype(np.float32)
    gup = inp["L1_ffn_gate_up_proj"]
    gub = inp["L1_ffn_gate_up_proj_bias"]
    W["wg5"] = np.stack([tileT(gup[e][:, 0::2]) for e in range(16)])
    W["wu5"] = np.stack([tileT(gup[e][:, 1::2]) for e in range(16)])
    W["bg_pp"] = np.stack([pp(gub[e][0::2]) for e in range(16)])
    W["bu_pp"] = np.stack([pp(gub[e][1::2]) for e in range(16)])
    W["wd"] = np.ascontiguousarray(inp["L1_ffn_down_proj"]).astype(np.float32)
    W["db"] = np.ascontiguousarray(
        inp["L1_ffn_down_proj_bias"]).astype(np.float32)
    return W


def prep_layer2(inp):
    W = dict(host_consts())
    waT = inp["L2_attn_W_attn"].T
    W["wq5"] = tileT(waT[:, 0:1024])
    W["wk5"] = tileT(waT[:, 1024:2048])
    W["wv5"] = tileT(waT[:, 2048:3072])
    ba = inp["L2_attn_b_attn"]
    W["bq_pp"] = pp(ba[0:1024])
    W["bk_pp"] = pp(ba[1024:2048])
    W["bv_pp"] = pp(ba[2048:3072])
    W["wo"] = np.ascontiguousarray(inp["L2_attn_W_proj"].T).astype(np.float32)
    W["bo_bc"] = bc(inp["L2_attn_b_proj"])
    W["rel_emb"] = np.ascontiguousarray(
        inp["L2_attn_rel_emb"]).astype(np.float32)
    W["ohrevT"] = host_ohrevT()
    W["alpha"] = np.asarray(inp["L2_ffn_alpha"],
                            dtype=np.float32).reshape(1, 1)
    W["w_gate"] = np.ascontiguousarray(inp["L2_ffn_w_gate"]).astype(np.float32)
    W["b_gate"] = np.ascontiguousarray(
        inp["L2_ffn_b_gate"]).astype(np.float32).reshape(1, 8)
    W["w1m5"] = tileT(inp["L2_ffn_W1m"])
    W["b1m_pp"] = pp(inp["L2_ffn_b1m"])
    W["w2m"] = np.ascontiguousarray(inp["L2_ffn_W2m"]).astype(np.float32)
    W["b2m_bc"] = bc(inp["L2_ffn_b2m"])
    W["w1e5"] = np.stack([tileT(inp["L2_ffn_W1e"][e]) for e in range(8)])
    W["b1e_pp"] = np.stack([pp(inp["L2_ffn_b1e"][e]) for e in range(8)])
    W["w2e"] = np.ascontiguousarray(inp["L2_ffn_W2e"]).astype(np.float32)
    W["b2e_bc"] = np.stack([bc(inp["L2_ffn_b2e"][e]) for e in range(8)])
    return W


def prep_layer3(inp):
    W = dict(host_consts())
    W["wq5"] = tileT(inp["L3_attn_WQ"].T)
    W["bq_pp"] = pp(inp["L3_attn_bQ"])
    W["wk5"] = tileT(inp["L3_attn_WK"].T)
    W["bk_pp"] = pp(inp["L3_attn_bK"])
    W["wv5"] = tileT(inp["L3_attn_WV"].T)
    W["bv_pp"] = pp(inp["L3_attn_bV"])
    W["wo"] = np.ascontiguousarray(inp["L3_attn_WO"].T).astype(np.float32)
    W["bo_bc"] = bc(inp["L3_attn_bO"])
    r3 = inp["L3_attn_rel_pos_table"]
    r3rev = np.ascontiguousarray(r3[::-1].T).astype(np.float32)
    W["r3rev"] = np.concatenate(
        [r3rev, np.zeros((16, 1), np.float32)], axis=1)
    W["router"] = np.ascontiguousarray(inp["L3_ffn_Wg"].T).astype(np.float32)
    W["wg5"] = np.stack(
        [tileT(inp["L3_ffn_We_gate"][e].T) for e in range(16)])
    W["wu5"] = np.stack([tileT(inp["L3_ffn_We_up"][e].T) for e in range(16)])
    W["wd"] = np.ascontiguousarray(
        inp["L3_ffn_We_down"].transpose(0, 2, 1)).astype(np.float32)
    W["wsg5"] = tileT(inp["L3_ffn_Ws_gate"].T)
    W["wsu5"] = tileT(inp["L3_ffn_Ws_up"].T)
    W["wsd"] = np.ascontiguousarray(inp["L3_ffn_Ws_down"].T).astype(np.float32)
    W["wsg"] = np.ascontiguousarray(
        inp["L3_ffn_Wsg"].T).astype(np.float32)
    return W



def declare_weights(P, Wnp, prefix):
    aps = {}
    for k, v in Wnp.items():
        dt = BF16 if v.dtype == BF else F32
        aps[k] = P.nc.dram_tensor(prefix + k, list(v.shape), dt,
                                  kind="ExternalInput").ap()
    return aps


def emit_layer(P, B, li, Waps, x_ap, y_ap, xa_ap, xb_ap):
    ffn = [lambda *a: ffn_l0(*a, AF.Gelu), ffn_l1, ffn_l2, ffn_l3][li]
    kind = f"l{li}"
    ffn(P, B, x_ap, xa_ap, Waps)
    P.barrier_reset()
    attention(P, B, xa_ap, xb_ap, Waps, kind)
    P.barrier_reset()
    ffn(P, B, xb_ap, y_ap, Waps)
    P.barrier_reset()


def build_layer_program(li, Wnp):
    P = PB()
    B = Bufs(P)
    nc = P.nc
    x = nc.dram_tensor("x", [T, D], F32, kind="ExternalInput").ap()
    y = nc.dram_tensor("y", [T, D], F32, kind="ExternalOutput").ap()
    xa = nc.dram_tensor("xa", [T, D], F32, kind="Internal").ap()
    xb = nc.dram_tensor("xb", [T, D], F32, kind="Internal").ap()
    Waps = declare_weights(P, Wnp, "w_")
    if li in (2, 3):
        Waps["rtab"] = nc.dram_tensor("rtab", [16, 4096], BF16,
                                      kind="Internal").ap()
    load_consts(P, B, Waps)
    emit_layer(P, B, li, Waps, x, y, xa, xb)
    P.finish()
    return P


_PREPS = [prep_layer0, prep_layer1, prep_layer2, prep_layer3]

FUSED = True


def build_full_program(Ws):
    P = PB()
    B = Bufs(P)
    nc = P.nc
    chain = [nc.dram_tensor("x", [T, D], F32, kind="ExternalInput").ap()]
    for i in range(11):
        chain.append(nc.dram_tensor(f"xm{i}", [T, D], F32,
                                    kind="Internal").ap())
    chain.append(nc.dram_tensor("y", [T, D], F32, kind="ExternalOutput").ap())
    Waps_all = []
    for li, Wnp in enumerate(Ws):
        aps = {}
        for k, v in Wnp.items():
            if k.startswith("c_"):
                continue
            dt = BF16 if v.dtype == BF else F32
            aps[k] = nc.dram_tensor(f"L{li}_{k}", list(v.shape), dt,
                                    kind="ExternalInput").ap()
        if li in (2, 3):
            aps["rtab"] = nc.dram_tensor(f"rtab{li}", [16, 4096], BF16,
                                         kind="Internal").ap()
        Waps_all.append(aps)
    cst = Ws[0]
    caps = {}
    for k in ("c_id", "c_flip", "c_cm4", "c_ones"):
        caps[k] = nc.dram_tensor(k, list(cst[k].shape), BF16,
                                 kind="ExternalInput").ap()
    load_consts(P, B, caps)
    for li in range(4):
        Waps = dict(Waps_all[li])
        Waps.update(caps)
        emit_layer(P, B, li, Waps, chain[3 * li], chain[3 * li + 3],
                   chain[3 * li + 1], chain[3 * li + 2])
    P.finish()
    return P


def kernel(**inp):
    x = np.ascontiguousarray(inp["L0_attn_x"]).astype(np.float32)
    xs = [np.ascontiguousarray(x[c]) for c in range(8)]
    Ws = [_PREPS[li](inp) for li in range(4)]
    if FUSED:
        P = build_full_program(Ws)
        base = {}
        for li in range(4):
            for k, v in Ws[li].items():
                if not k.startswith("c_"):
                    base[f"L{li}_{k}"] = v
        for k in ("c_id", "c_flip", "c_cm4", "c_ones"):
            base[k] = Ws[0][k]
        in_maps = [dict(base, x=xs[c]) for c in range(8)]
        res = run_bass_kernel_spmd(P.nc, in_maps, core_ids=list(range(8)))
        out = np.stack([np.asarray(r["y"]) for r in res.results])
        P.st.close()
        return out.astype(np.float32)
    for li in range(4):
        Wnp = Ws[li]
        P = build_layer_program(li, Wnp)
        in_maps = []
        for c in range(8):
            m = {"w_" + k: v for k, v in Wnp.items()}
            m["x"] = xs[c]
            in_maps.append(m)
        res = run_bass_kernel_spmd(P.nc, in_maps, core_ids=list(range(8)))
        xs = [np.asarray(r["y"]) for r in res.results]
        P.st.close()
    return np.stack(xs).astype(np.float32)


if __name__ == "__main__":
    pass
